# version 04
# baseline (speedup 1.0000x reference)
import numpy as np
import ml_dtypes
from contextlib import ExitStack

import concourse.bass as bass
import concourse.mybir as mybir
from concourse.bass_utils import run_bass_kernel_spmd

F32 = mybir.dt.float32
BF16 = mybir.dt.bfloat16
AF = mybir.ActivationFunctionType
ALU = mybir.AluOpType
AX = mybir.AxisListType

NCORE = 8
TTOK = 4096
D = 1024
DFF = 4096
NT = 8
TILE = 512
NM = 4
KC = 8
FT = 32
NXR = 6
WQN = ["q", "k", "g", "v", "o"]


class Sem:
    __slots__ = ("h", "v", "name")

    def __init__(self, h, name):
        self.h = h
        self.v = 0
        self.name = name

    def inc(self, n=1):
        self.v += n
        return self.v


def build_program(nt=NT, dbg=False):
    nc = bass.Bass("TRN2", target_bir_lowering=False)
    dt = nc.dram_tensor
    a_x = dt("x", [TTOK, D], F32, kind="ExternalInput").ap()
    a_w = {w: dt(f"w{w}T", [D, D], F32, kind="ExternalInput").ap() for w in WQN}
    a_up = dt("wupT", [D, DFF], F32, kind="ExternalInput").ap()
    a_dn = dt("wdnT", [DFF, D], F32, kind="ExternalInput").ap()
    a_ld = dt("ld", [1, 1], F32, kind="ExternalInput").ap()
    a_idf = dt("identf", [128, 128], F32, kind="ExternalInput").ap()
    a_ones = dt("onesrow", [1, 128], F32, kind="ExternalInput").ap()
    a_onesb = dt("onescol", [128, 1], BF16, kind="ExternalInput").ap()
    a_cst = dt("cst", [128, 4], F32, kind="ExternalInput").ap()
    a_mE = dt("maskE", [128, 128], F32, kind="ExternalInput").ap()
    a_mZ = dt("maskZ", [128, 128], F32, kind="ExternalInput").ap()
    a_out = dt("out", [TTOK, D], F32, kind="ExternalOutput").ap()
    s_w = {w: dt(f"s_w{w}", [128, KC, D], BF16, kind="ExternalOutput").ap() for w in WQN}
    s_up = dt("s_up", [128, KC, FT, 128], BF16, kind="ExternalOutput").ap()
    s_dn = dt("s_dn", [128, FT, D], BF16, kind="ExternalOutput").ap()
    s_misc = dt("s_misc", [1, 4], F32, kind="ExternalOutput").ap()
    a_dbg = {}
    if dbg:
        for nm, sh, dty in [
            ("d_xnfm", [128, 4096], BF16), ("d_qfm", [128, 4096], BF16),
            ("d_kfm", [128, 4096], BF16), ("d_vtm", [128, 4096], BF16),
            ("d_gO", [128, 4096], BF16), ("d_hfm", [128, 16384], BF16),
            ("d_stm", [128, 512], BF16), ("d_acc", [128, 4096], F32),
            ("d_xnffn", [128, 4096], BF16), ("d_mv", [128, 8], F32),
            ("d_s12", [128, 8], F32), ("d_maskT", [128, 128], F32),
            ("d_tc", [128, 4], F32), ("d_ln", [128, 12], F32),
        ]:
            a_dbg[nm] = dt(nm, sh, dty, kind="ExternalOutput").ap()

    with ExitStack() as ctx:
        E_ = ctx.enter_context
        sb = nc.sbuf_tensor
        ps_ = nc.psum_tensor
        c_idf = E_(sb("c_idf", [128, 128], F32))
        c_ones = E_(sb("c_ones", [1, 128], F32))
        c_onesb = E_(sb("c_onesb", [128, 1], BF16))
        c_cst = E_(sb("c_cst", [128, 4], F32))
        c_mE = E_(sb("c_mE", [128, 128], F32))
        c_mZ = E_(sb("c_mZ", [128, 128], F32))
        b_maskT = E_(sb("b_maskT", [128, 128], F32))
        b_mtmp = E_(sb("b_mtmp", [128, 128], F32))
        b_ld = E_(sb("b_ld", [1, 2], F32))
        xr = [E_(sb(f"xr{i}", [128, D], F32)) for i in range(NXR)]
        wsl = [E_(sb(f"wsl{i}", [128, 4096], BF16)) for i in range(2)]
        upsl = [E_(sb(f"upsl{i}", [128, 4096], BF16)) for i in range(2)]
        dnsl = [E_(sb(f"dnsl{i}", [128, 4096], BF16)) for i in range(2)]
        xnr = [E_(sb(f"xnr{i}", [128, D], F32)) for i in range(2)]
        xn_fm = E_(sb("xn_fm", [128, 4096], BF16))
        xnffn = E_(sb("xnffn", [128, 4096], BF16))
        q_fm = E_(sb("q_fm", [128, 4096], BF16))
        k_fm = E_(sb("k_fm", [128, 4096], BF16))
        v_tm = E_(sb("v_tm", [128, 4096], BF16))
        sig_fm = E_(sb("sig_fm", [128, 4096], BF16))
        h_fm = E_(sb("h_fm", [128, 16384], BF16))
        sqr = [E_(sb(f"sqr{i}", [128, 512], BF16)) for i in range(2)]
        stm = E_(sb("stm", [128, 512], BF16))
        acc = E_(sb("acc", [128, 4096], F32))
        xxnr = [E_(sb(f"xxnr{i}", [128, D], F32)) for i in range(2)]
        outst = [E_(sb(f"outst{i}", [128, D], F32)) for i in range(2)]
        b_bnst = E_(sb("b_bnst", [128, 12], F32))
        b_mv = E_(sb("b_mv", [128, 8], F32))
        b_stage = E_(sb("b_stage", [128, 8], F32))
        b_sqo = E_(sb("b_sqo", [128, 8], F32))
        b_rstd = E_(sb("b_rstd", [128, 4], F32))
        b_s1 = E_(sb("b_s1", [128, 4], F32))
        b_b1 = E_(sb("b_b1", [128, 4], F32))
        b_a3 = E_(sb("b_a3", [128, 4], F32))
        b_rms2 = E_(sb("b_rms2", [128, 4], F32))
        b_s2 = E_(sb("b_s2", [128, 4], F32))
        b_t4 = E_(sb("b_t4", [128, 4], F32))
        b_s1row = E_(sb("b_s1row", [1, 512], F32))
        b_sumrow = E_(sb("b_sumrow", [1, 512], F32))
        b_acol = E_(sb("b_acol", [128, 32], F32))
        b_asum = E_(sb("b_asum", [128, 1], F32))
        b_asum2 = E_(sb("b_asum2", [128, 1], F32))
        b_trow2 = E_(sb("b_trow2", [1, 4], F32))
        b_arow = E_(sb("b_arow", [1, 128], F32))
        b_trow = E_(sb("b_trow", [1, 4], F32))
        b_tc = {m: E_(sb(f"b_tc_{m}", [128, 2], F32)) for m in ("up", "dn")}
        b_junk = E_(sb("b_junk", [128, 8], F32))
        p = [E_(ps_(f"p{i}", [128, 512], F32)) for i in range(8)]

        SX = Sem(E_(nc.semaphore("SX")), "SX")
        SO = Sem(E_(nc.semaphore("SO")), "SO")
        SWS = Sem(E_(nc.semaphore("SWS")), "SWS")
        SWG = Sem(E_(nc.semaphore("SWG")), "SWG")
        SWC = Sem(E_(nc.semaphore("SWC")), "SWC")
        SWL = Sem(E_(nc.semaphore("SWL")), "SWL")
        SPE = Sem(E_(nc.semaphore("SPE")), "SPE")
        SDV = Sem(E_(nc.semaphore("SDV")), "SDV")
        SAC = Sem(E_(nc.semaphore("SAC")), "SAC")

        prog = {k: [] for k in ("t", "v", "s", "y", "g")}
        lastw = {k: {} for k in prog}

        def E(eng, fn, waits=(), inc=None, n=1):
            wl = []
            for s, v in waits:
                if v is None or v <= 0:
                    continue
                if lastw[eng].get(s.name, -1) < v:
                    wl.append((s.h, v))
                    lastw[eng][s.name] = v
            val = inc.inc(n) if inc is not None else None
            prog[eng].append((wl, fn, inc.h if inc is not None else None, n))
            return val

        bankf = {i: None for i in range(8)}

        def acq(i):
            r = bankf[i]
            return [r] if r is not None else []

        def pad(eng, cyc=512):
            E(eng, lambda e: e.nop(cycle_cnt=cyc))

        def mile(eng, inc, n=1):
            pad(eng)
            if eng == "v":
                return E("v", lambda e: e.tensor_copy(b_junk[:, 0:2], b_junk[:, 2:4]),
                         inc=inc, n=n)
            if eng == "s":
                return E("s", lambda e: e.copy(b_junk[:, 4:6], b_junk[:, 6:8]),
                         inc=inc, n=n)
            raise ValueError(eng)

        E("s", lambda e: e.memzero(b_junk[:, 4:8]))
        E("v", lambda e: e.memzero(b_junk[:, 0:4]))
        for f_ in (AF.Sigmoid, AF.Ln, AF.Exp, AF.Sqrt, AF.Gelu, AF.Square,
                   AF.Sign, AF.Abs, AF.Copy):
            E("s", lambda e, f_=f_: e.activation(b_junk[:, 4:6], b_junk[:, 6:8], f_))
        warm_v = mile("s", SAC)

        const_pairs = [
            (c_idf[:, :], a_idf[:, :]), (c_ones[:, :], a_ones[:, :]),
            (c_onesb[:, :], a_onesb[:, :]), (c_cst[:, :], a_cst[:, :]),
            (c_mE[:, :], a_mE[:, :]), (c_mZ[:, :], a_mZ[:, :]),
            (b_ld[0:1, 0:1], a_ld[:, :]),
        ]
        v_consts = None
        for d_, s_ in const_pairs:
            v_consts = E("y", lambda e, d_=d_, s_=s_: e.dma_start(d_, s_),
                         inc=SWS, n=16)

        xfree = {i: None for i in range(NXR)}
        xload = {}
        for m in range(NM):
            sl = m % NXR
            xload[(0, m)] = E("s", lambda e, sl=sl, m=m: e.dma_start(
                xr[sl][:], a_x[m * 128:(m + 1) * 128, :]), inc=SX, n=16)

        castv = {}
        for w in WQN:
            castv[w] = E("g", lambda e, w=w: e.dma_start(
                s_w[w][:, :, :], a_w[w].rearrange("(k p) i -> p k i", p=128)),
                inc=SWC, n=16)

        E("s", lambda e: e.activation(b_ld[0:1, 1:2], b_ld[0:1, 0:1], AF.Sigmoid),
          waits=[(SWS, v_consts)])
        pad("s")
        E("s", lambda e: e.activation(b_ld[0:1, 1:2], b_ld[0:1, 1:2], AF.Ln))
        vac = mile("s", SAC)
        vpe = E("t", lambda e: e.matmul(p[4][:, 0:1], c_ones[0:1, :], b_ld[0:1, 1:2],
                                        start=True, stop=True),
                waits=[(SAC, vac)], inc=SPE)
        E("v", lambda e: e.tensor_copy(b_t4[:, 0:1], p[4][:, 0:1]),
          waits=[(SPE, vpe)])
        vdv = mile("v", SDV)
        bankf[4] = (SDV, vdv)
        vdv = E("v", lambda e: e.tensor_scalar(b_maskT[:, :], c_mE[:, :], b_t4[:, 0:1],
                                               None, op0=ALU.mult),
                waits=[(SDV, vdv)], inc=SDV)
        vac2 = E("s", lambda e: e.activation(b_mtmp[:, :], b_maskT[:, :], AF.Exp),
                 waits=[(SDV, vdv)], inc=SAC)
        vmask = E("v", lambda e: e.tensor_tensor(b_maskT[:, :], b_mtmp[:, :],
                                                 c_mZ[:, :], op=ALU.mult),
                  waits=[(SAC, vac2)], inc=SDV)

        tfree = {0: None, 1: None}
        absfree = {0: None, 1: None, 2: None, 3: None}
        dvfree = {0: None, 1: None}

        def ternarize(mat):
            for qq in range(32):
                sl = qq % 2
                if mat == "up":
                    c, j = qq // 4, qq % 4
                    src = a_up[c * 128:(c + 1) * 128, j * D:(j + 1) * D]
                else:
                    src = a_dn[qq * 128:(qq + 1) * 128, :]
                w_ = [tfree[sl]] if tfree[sl] is not None else []
                if qq == 0:
                    w_ = w_ + [(SWC, castv["o"])]
                lv = E("g", lambda e, sl=sl, src=src: e.dma_start(xr[4 + sl][:], src),
                       waits=w_, inc=SWL, n=16)
                av = E("v", lambda e, sl=sl, qq=qq: e.reduce_sum(
                    b_acol[:, qq:qq + 1], xr[4 + sl][:], axis=AX.X,
                    apply_absolute_value=True), waits=[(SWL, lv)], inc=SDV)
                tfree[sl] = (SDV, av)
            vv2 = E("v", lambda e: e.reduce_sum(b_asum[:, :], b_acol[:, :],
                                                axis=AX.X),
                    waits=[(SDV, av)], inc=SDV)
            gv = E("g", lambda e: e.dma_start(b_arow[0:1, :], b_asum[:, 0:1]),
                   waits=[(SDV, vv2)], inc=SWG, n=16)
            vv2 = E("v", lambda e: e.reduce_sum(b_trow2[0:1, 0:1], b_arow[0:1, :],
                                                axis=AX.X), waits=[(SWG, gv)],
                    inc=SDV)
            vv2 = E("v", lambda e: e.tensor_scalar(
                b_trow2[0:1, 1:2], b_trow2[0:1, 0:1], 0.5 / (D * DFF), None,
                op0=ALU.mult), waits=[(SDV, vv2)], inc=SDV)
            vv2 = E("v", lambda e: e.tensor_scalar(
                b_trow2[0:1, 2:3], b_trow2[0:1, 0:1], -0.5 / (D * DFF), None,
                op0=ALU.mult), waits=[(SDV, vv2)], inc=SDV)
            mo = 0 if mat == "up" else 2
            gv = E("g", lambda e, mo=mo: e.dma_start(
                s_misc[0:1, mo:mo + 2], b_trow2[0:1, 1:3]),
                waits=[(SDV, vv2)], inc=SWG, n=16)
            tcv2 = E("g", lambda e, mat=mat, mo=mo: e.dma_start(
                b_tc[mat][:, :], s_misc.partition_broadcast(128)[:, 0, mo:mo + 2]),
                waits=[(SWG, gv)], inc=SWG, n=16)
            stv = None
            lastsign = {0: None, 1: None}
            for qq in range(32):
                sl = qq % 2
                if mat == "up":
                    c, j = qq // 4, qq % 4
                    src = a_up[c * 128:(c + 1) * 128, j * D:(j + 1) * D]
                    dst = s_up[:, c, j * 8:(j + 1) * 8, :]
                else:
                    src = a_dn[qq * 128:(qq + 1) * 128, :]
                    dst = s_dn[:, qq, :]
                w_ = [tfree[sl]] if tfree[sl] is not None else []
                lv = E("g", lambda e, sl=sl, src=src: e.dma_start(xr[4 + sl][:], src),
                       waits=w_, inc=SWL, n=16)
                p1 = upsl[0][:, sl * 2048:sl * 2048 + 1024]
                p2 = upsl[1][:, sl * 2048:sl * 2048 + 1024]
                wt = dnsl[0][:, sl * 2048:sl * 2048 + 1024]
                E("s", lambda e, sl=sl, p1=p1, mat=mat: e.sign(
                    p1, xr[4 + sl][:], bias=b_tc[mat][:, 1:2]),
                    waits=[(SWL, lv), (SWG, tcv2)])
                av = E("s", lambda e, sl=sl, p2=p2, mat=mat: e.sign(
                    p2, xr[4 + sl][:], bias=b_tc[mat][:, 0:1]), inc=SAC)
                tfree[sl] = (SAC, av)
                lastsign[sl] = (SAC, av)
                w_ = [(SAC, av)] + ([dvfree[sl]] if dvfree[sl] is not None else [])
                E("v", lambda e, p1=p1, p2=p2, wt=wt: e.tensor_tensor(
                    wt, p1, p2, op=ALU.add), waits=w_)
                dv = E("v", lambda e, wt=wt: e.tensor_scalar(
                    wt, wt, 0.5, None, op0=ALU.mult), inc=SDV)
                if mat == "up":
                    stv = E("g", lambda e, dst=dst, wt=wt: e.dma_start(
                        dst, wt.rearrange("p (j f) -> p j f", f=128)),
                        waits=[(SDV, dv)], inc=SWG, n=16)
                else:
                    stv = E("g", lambda e, dst=dst, wt=wt: e.dma_start(dst, wt),
                            waits=[(SDV, dv)], inc=SWG, n=16)
                dvfree[sl] = (SWG, stv)
            return stv, lastsign

        up_ready, _ = ternarize("up")
        dn_ready, lastsign = ternarize("dn")
        xfree[4] = lastsign[0]
        xfree[5] = lastsign[1]

        if dbg:
            E("s", lambda e: e.dma_start(a_dbg["d_maskT"][:, :], b_maskT[:, :]),
              waits=[(SDV, vmask)], inc=SO, n=16)
            E("s", lambda e: e.dma_start(a_dbg["d_tc"][:, 0:2], b_tc["up"][:, :]),
              inc=SO, n=16)
            E("s", lambda e: e.dma_start(a_dbg["d_tc"][:, 2:4], b_tc["dn"][:, :]),
              inc=SO, n=16)

        wslfree = {0: None, 1: None}
        upslfree = {0: (SDV, SDV.v), 1: (SDV, SDV.v)}
        dnslfree = {0: (SWG, SWG.v), 1: None}
        sqfree = {0: None, 1: None}
        xnfree = {0: None, 1: None}
        xxnfree = {0: None, 1: None}
        outfree = {0: None, 1: None}
        prev_a1 = None
        prev_s1t = None

        projrot = [0]

        def mmgroup(bk, lhsT_fn, rhs_fn, waits, nk=8, width=512):
            def fn(e):
                ins = None
                for k in range(nk):
                    ins = e.matmul(p[bk][:, 0:width], lhsT_fn(k), rhs_fn(k),
                                   start=(k == 0), stop=(k == nk - 1))
                return ins
            return E("t", fn, waits=waits, inc=SPE)

        def emit_tile(t):
            nonlocal prev_a1, prev_s1t
            r0 = t * TILE
            def emit_xload(tt, m):
                sl = (tt * NM + m) % NXR
                w_ = [xfree[sl]] if xfree[sl] is not None else []
                xload[(tt, m)] = E("s", lambda e, sl=sl, tt=tt, m=m: e.dma_start(
                    xr[sl][:],
                    a_x[tt * TILE + m * 128:tt * TILE + (m + 1) * 128, :]),
                    waits=w_, inc=SX, n=16)
            if t + 1 < nt:
                for m in range(2):
                    emit_xload(t + 1, m)

            halves = [(w, H) for w in WQN for H in range(2)]
            wshalf = {}

            def emit_whalf(j):
                w, H = halves[j]
                sl = (t * 10 + j) % 2
                w_ = [(SWC, castv[w])] + ([wslfree[sl]] if wslfree[sl] is not None else [])
                wshalf[(w, H)] = (sl, E("y", lambda e, sl=sl, w=w, H=H: e.dma_start(
                    wsl[sl][:].rearrange("p (k i) -> p k i", i=512),
                    s_w[w][:, :, H * 512:(H + 1) * 512]), waits=w_, inc=SWS, n=16))

            def whalf_done(j, rec):
                sl = (t * 10 + j) % 2
                wslfree[sl] = rec
                if j + 2 < 10:
                    emit_whalf(j + 2)
            emit_whalf(0)
            emit_whalf(1)

            upsup = {}
            dnsup = {}

            def emit_upsup(cc):
                sl = cc % 2
                w_ = [(SWG, up_ready)] + ([upslfree[sl]] if upslfree[sl] is not None else [])
                upsup[cc] = E("g", lambda e, sl=sl, cc=cc: e.dma_start(
                    upsl[sl][:].rearrange("p (k j f) -> p k j f", j=4, f=128),
                    s_up[:, :, cc * 4:(cc + 1) * 4, :]), waits=w_, inc=SWG, n=16)

            def emit_dnsup(cc):
                sl = cc % 2
                w_ = [(SWG, dn_ready)] + ([dnslfree[sl]] if dnslfree[sl] is not None else [])
                dnsup[cc] = E("g", lambda e, sl=sl, cc=cc: e.dma_start(
                    dnsl[sl][:].rearrange("p (j i) -> p j i", j=4),
                    s_dn[:, cc * 4:(cc + 1) * 4, :]), waits=w_, inc=SWG, n=16)
            emit_upsup(0)
            emit_upsup(1)
            emit_dnsup(0)
            emit_dnsup(1)

            sq_ac = {}
            for m in range(NM):
                sl = (t * NM + m) % NXR
                rsl = m % 2
                w_ = [(SX, xload[(t, m)])]
                if xnfree[rsl] is not None:
                    w_ = w_ + [xnfree[rsl]]
                E("s", lambda e, sl=sl, rsl=rsl, m=m: e.activation(
                    xnr[rsl][:], xr[sl][:], AF.Copy,
                    accum_out=b_bnst[:, m:m + 1]), waits=w_)
                E("s", lambda e, sl=sl, rsl=rsl, m=m: e.activation(
                    xnr[rsl][:], xr[sl][:], AF.Square,
                    accum_out=b_bnst[:, 4 + m:5 + m]))
                sq_ac[m] = mile("s", SAC) if m >= NM - 2 else None
            vv = E("v", lambda e: e.tensor_scalar(
                b_mv[:, 0:4], b_bnst[:, 0:4], 1.0 / D, None, op0=ALU.mult),
                waits=[(SAC, sq_ac[NM - 1])], inc=SDV)
            vv = E("v", lambda e: e.tensor_scalar(
                b_mv[:, 4:8], b_bnst[:, 4:8], 1.0 / D, None, op0=ALU.mult),
                inc=SDV)
            vv = E("v", lambda e: e.tensor_tensor(
                b_t4[:, 0:4], b_mv[:, 0:4], b_mv[:, 0:4], op=ALU.mult),
                waits=[(SDV, vv)], inc=SDV)
            vv = E("v", lambda e: e.tensor_tensor(
                b_mv[:, 4:8], b_mv[:, 4:8], b_t4[:, 0:4], op=ALU.subtract),
                waits=[(SDV, vv)], inc=SDV)
            w_ = [(SDV, vv)] + ([prev_a1] if prev_a1 is not None else [])
            vv = E("v", lambda e: e.tensor_scalar(
                b_stage[:, 0:4], b_mv[:, 4:8], c_cst[:, 0:1], None, op0=ALU.add),
                waits=w_, inc=SDV)
            vv = E("v", lambda e: e.reciprocal(b_t4[:, 0:4], b_stage[:, 0:4]),
                   waits=[(SDV, vv)], inc=SDV)
            v1 = E("v", lambda e: e.tensor_tensor(
                b_stage[:, 4:8], b_mv[:, 4:8], b_t4[:, 0:4], op=ALU.mult),
                waits=[(SDV, vv)], inc=SDV)
            a1 = E("s", lambda e: e.activation(b_sqo[:, :], b_stage[:, :], AF.Sqrt),
                   waits=[(SDV, v1)], inc=SAC)
            prev_a1 = (SAC, a1)
            vv = E("v", lambda e: e.reciprocal(b_rstd[:, :], b_sqo[:, 0:4]),
                   waits=[(SAC, a1)], inc=SDV)
            vv = E("v", lambda e: e.tensor_scalar(b_t4[:, :], b_sqo[:, 4:8], 1e-8,
                                                  None, op0=ALU.add), inc=SDV)
            w_ = [(SDV, vv)] + ([prev_s1t] if prev_s1t is not None else [])
            E("v", lambda e: e.reciprocal(b_s1[:, :], b_t4[:, :]), waits=w_)
            vv = E("v", lambda e: e.tensor_tensor(
                b_b1[:, 0:4], b_mv[:, 0:4], b_rstd[:, 0:4], op=ALU.mult), inc=SDV)
            vv = E("v", lambda e: e.tensor_scalar(b_b1[:, :], b_b1[:, :], -1.0, None,
                                                  op0=ALU.mult),
                   waits=[(SDV, vv)], inc=SDV)
            v2 = E("v", lambda e: e.tensor_scalar(b_a3[:, :], b_rstd[:, :], 1.0, None,
                                                  op0=ALU.add),
                   waits=[(SDV, vv)], inc=SDV)

            v3 = {}
            v4 = {}
            for m in range(NM):
                sl = (t * NM + m) % NXR
                rsl = m % 2
                w_ = [(SAC, sq_ac[2 + rsl])]
                if xnfree[rsl] is not None:
                    w_ = w_ + [xnfree[rsl]]
                v3[m] = E("v", lambda e, sl=sl, rsl=rsl, m=m: e.tensor_scalar(
                    xnr[rsl][:], xr[sl][:], b_rstd[:, m:m + 1], b_b1[:, m:m + 1],
                    op0=ALU.mult, op1=ALU.add), waits=w_, inc=SDV)
                bk = (0, 1) if m % 2 == 0 else (2, 3)
                w_ = [(SDV, v3[m])] + acq(bk[0]) + acq(bk[1])

                def tfn(e, rsl=rsl, bk=bk):
                    ins = None
                    for c in range(8):
                        b_ = p[bk[0]] if c < 4 else p[bk[1]]
                        ins = e.transpose(b_[:, (c % 4) * 128:(c % 4) * 128 + 128],
                                          xnr[rsl][:, c * 128:(c + 1) * 128],
                                          c_idf[:, :])
                    return ins
                tv = E("t", tfn, waits=w_, inc=SPE)
                xnfree[rsl] = (SPE, tv)

                def evfn(e, m=m, bk=bk):
                    e.tensor_copy(
                        xn_fm[:, :].rearrange("p (c u) -> p c u", u=512)[:, 0:4, m * 128:(m + 1) * 128],
                        p[bk[0]][:, :].rearrange("p (c u) -> p c u", u=128))
                    return e.tensor_copy(
                        xn_fm[:, :].rearrange("p (c u) -> p c u", u=512)[:, 4:8, m * 128:(m + 1) * 128],
                        p[bk[1]][:, :].rearrange("p (c u) -> p c u", u=128))
                v4[m] = E("v", evfn, waits=[(SPE, tv)], inc=SDV)
                bankf[bk[0]] = (SDV, v4[m])
                bankf[bk[1]] = (SDV, v4[m])
            xn_done = v4[NM - 1]

            for wi, w in enumerate(("q", "k", "g")):
                for H in range(2):
                    sl, wv = wshalf[(w, H)]
                    for i4 in range(4):
                        ig = H * 4 + i4
                        bk = projrot[0] % 2
                        projrot[0] += 1
                        gv = mmgroup(
                            bk,
                            lambda k, sl=sl, i4=i4: wsl[sl][:, k * 512 + i4 * 128:k * 512 + i4 * 128 + 128],
                            lambda k: xn_fm[:, k * 512:(k + 1) * 512],
                            waits=[(SWS, wv), (SDV, xn_done)] + acq(bk))
                        if i4 == 3:
                            whalf_done(wi * 2 + H, (SPE, gv))
                        if w == "g":
                            av = E("s", lambda e, ig=ig, bk=bk: e.activation(
                                sig_fm[:, ig * 512:(ig + 1) * 512], p[bk][:, 0:512],
                                AF.Sigmoid), waits=[(SPE, gv)], inc=SAC)
                            bankf[bk] = (SAC, av)
                        else:
                            dst = q_fm if w == "q" else k_fm
                            dv = E("v", lambda e, dst=dst, ig=ig, bk=bk: e.tensor_copy(
                                dst[:, ig * 512:(ig + 1) * 512], p[bk][:, 0:512]),
                                waits=[(SPE, gv)], inc=SDV)
                            bankf[bk] = (SDV, dv)
            qk_done = SDV.v
            g_done = SAC.v

            for H in range(2):
                sl, wv = wshalf[("v", H)]
                for m in range(NM):
                    bk = 2 + (H * NM + m) % 2
                    gv = mmgroup(
                        bk,
                        lambda k, m=m: xn_fm[:, k * 512 + m * 128:k * 512 + m * 128 + 128],
                        lambda k, sl=sl: wsl[sl][:, k * 512:(k + 1) * 512],
                        waits=[(SWS, wv)] + acq(bk))
                    if m == NM - 1:
                        whalf_done(6 + H, (SPE, gv))
                    dv = E("v", lambda e, m=m, H=H, bk=bk: e.tensor_copy(
                        v_tm[:, m * 1024 + H * 512:m * 1024 + H * 512 + 512],
                        p[bk][:, 0:512]), waits=[(SPE, gv)], inc=SDV)
                    bankf[bk] = (SDV, dv)

            v7 = {}
            for c in range(NM):
                bk = 4 + c % 2
                gv = mmgroup(
                    bk,
                    lambda k, c=c: k_fm[:, k * 512 + c * 128:k * 512 + c * 128 + 128],
                    lambda k, c=c: q_fm[:, k * 512 + c * 128:k * 512 + c * 128 + 128],
                    waits=[(SDV, qk_done)] + acq(bk), width=128)
                v7[c] = E("v", lambda e, c=c, bk=bk: e.tensor_tensor(
                    stm[:, c * 128:(c + 1) * 128], p[bk][:, 0:128], b_maskT[:, :],
                    op=ALU.mult), waits=[(SPE, gv), (SDV, vmask)], inc=SDV)
                bankf[bk] = (SDV, v7[c])
            stm_done = mile("v", SDV)

            for j in range(8):
                bk = 6 + j % 2

                def ofn(e, j=j, bk=bk):
                    ins = None
                    for c in range(4):
                        ins = e.matmul(p[bk][:, c * 128:(c + 1) * 128],
                                       v_tm[:, c * 1024 + j * 128:c * 1024 + j * 128 + 128],
                                       stm[:, c * 128:(c + 1) * 128],
                                       start=True, stop=True)
                    return ins
                gv = E("t", ofn, waits=[(SDV, stm_done)] + acq(bk), inc=SPE)
                dv = E("v", lambda e, j=j, bk=bk: e.tensor_tensor(
                    sig_fm[:, j * 512:(j + 1) * 512], sig_fm[:, j * 512:(j + 1) * 512],
                    p[bk][:, 0:512], op=ALU.mult),
                    waits=[(SPE, gv), (SAC, g_done)], inc=SDV)
                bankf[bk] = (SDV, dv)
            gO_done = SDV.v

            def s1t(e):
                ins = None
                for m in range(NM):
                    ins = e.transpose(p[4][0:1, m * 128:(m + 1) * 128],
                                      b_s1[:, m:m + 1], c_idf[:, :])
                return ins
            tv = E("t", s1t, waits=[(SDV, v2)] + acq(4), inc=SPE)
            prev_s1t = (SPE, tv)
            E("v", lambda e: e.tensor_copy(b_s1row[:, :], p[4][0:1, 0:512]),
              waits=[(SPE, tv)])
            v9 = mile("v", SDV)
            tv = E("t", lambda e: e.matmul(p[4][:, 0:512], c_ones[0:1, :],
                                           b_s1row[0:1, :], start=True, stop=True),
                   waits=[(SDV, v9)], inc=SPE)
            v10 = None
            for k in range(KC):
                v10 = E("v", lambda e, k=k: e.tensor_tensor(
                    xnffn[:, k * 512:(k + 1) * 512], xn_fm[:, k * 512:(k + 1) * 512],
                    p[4][:, 0:512], op=ALU.mult), waits=[(SPE, tv)], inc=SDV)
            bankf[4] = (SDV, v10)

            gelu_v = {}
            sumsq_pe = {}

            def emit_sumsq(f):
                w_ = [(SAC, gelu_v[f][1])] + (acq(5) if f == 0 else [])
                pv = E("t", lambda e, f=f: e.matmul(
                    p[5][0:1, 0:512], c_onesb[:, :], sqr[f % 2][:],
                    start=(f == 0), stop=(f == FT - 1)), waits=w_, inc=SPE)
                sumsq_pe[f] = pv
                sqfree[f % 2] = (SPE, pv)

            for f in range(FT):
                cc = f // 4
                bk = projrot[0] % 2
                projrot[0] += 1
                gv = mmgroup(
                    bk,
                    lambda k, cc=cc, f=f: upsl[cc % 2][:, k * 512 + (f % 4) * 128:k * 512 + (f % 4) * 128 + 128],
                    lambda k: xnffn[:, k * 512:(k + 1) * 512],
                    waits=[(SWG, upsup[cc]), (SDV, v10)] + acq(bk))
                if f % 4 == 3:
                    upslfree[cc % 2] = (SPE, gv)
                    if cc + 2 < 8:
                        emit_upsup(cc + 2)
                av1 = E("s", lambda e, f=f, bk=bk: e.activation(
                    h_fm[:, f * 512:(f + 1) * 512], p[bk][:, 0:512], AF.Gelu),
                    waits=[(SPE, gv)], inc=SAC)
                bankf[bk] = (SAC, av1)
                w_ = [sqfree[f % 2]] if sqfree[f % 2] is not None else []
                av2 = E("s", lambda e, f=f: e.activation(
                    sqr[f % 2][:], h_fm[:, f * 512:(f + 1) * 512], AF.Square),
                    waits=w_, inc=SAC)
                gelu_v[f] = (av1, av2)
                if f >= 2:
                    emit_sumsq(f - 2)
            emit_sumsq(FT - 2)
            emit_sumsq(FT - 1)
            h_done = gelu_v[FT - 1][0]

            v12 = {}
            for m in range(NM):
                sl = (t * NM + m) % NXR
                rsl = m % 2
                w_ = [xxnfree[rsl]] if xxnfree[rsl] is not None else []
                xv = E("v", lambda e, sl=sl, rsl=rsl, m=m: e.tensor_scalar(
                    xxnr[rsl][:], xr[sl][:], b_a3[:, m:m + 1], b_b1[:, m:m + 1],
                    op0=ALU.mult, op1=ALU.add), waits=w_, inc=SDV)
                xfree[sl] = (SDV, xv)
                if m < 2 and t + 1 < nt:
                    emit_xload(t + 1, m + 2)
                for H in range(2):
                    sl_w, wv = wshalf[("o", H)]
                    bk = 2 + (m * 2 + H) % 2
                    gv = mmgroup(
                        bk,
                        lambda k, m=m: sig_fm[:, k * 512 + m * 128:k * 512 + m * 128 + 128],
                        lambda k, sl_w=sl_w: wsl[sl_w][:, k * 512:(k + 1) * 512],
                        waits=[(SWS, wv), (SDV, gO_done)] + acq(bk))
                    if m == NM - 1:
                        whalf_done(8 + H, (SPE, gv))
                    dv = E("v", lambda e, m=m, H=H, bk=bk, rsl=rsl: e.tensor_tensor(
                        acc[:, m * 1024 + H * 512:m * 1024 + H * 512 + 512],
                        p[bk][:, 0:512], xxnr[rsl][:, H * 512:(H + 1) * 512],
                        op=ALU.add), waits=[(SPE, gv), (SDV, xv)], inc=SDV)
                    bankf[bk] = (SDV, dv)
                    v12[(m, H)] = dv
                xxnfree[rsl] = (SDV, v12[(m, 1)])

            E("v", lambda e: e.tensor_copy(b_sumrow[:, :], p[5][0:1, 0:512]),
              waits=[(SPE, sumsq_pe[FT - 1])])
            v11 = mile("v", SDV)
            bankf[5] = (SDV, v11)

            def s2t(e):
                ins = None
                for m in range(NM):
                    ins = e.transpose(p[4][:, m:m + 1],
                                      b_sumrow[0:1, m * 128:(m + 1) * 128],
                                      c_idf[0:1, 0:1])
                return ins
            tv = E("t", s2t, waits=[(SDV, v11)] + acq(4), inc=SPE)
            E("s", lambda e: e.activation(b_rms2[:, :], p[4][:, 0:4], AF.Sqrt,
                                          scale=1.0 / DFF), waits=[(SPE, tv)])
            a4 = mile("s", SAC)
            bankf[4] = (SAC, a4)
            vv = E("v", lambda e: e.tensor_scalar(b_t4[:, :], b_rms2[:, :], 1e-8,
                                                  None, op0=ALU.add),
                   waits=[(SAC, a4)], inc=SDV)
            s2v = E("v", lambda e: e.reciprocal(b_s2[:, :], b_t4[:, :]),
                    waits=[(SDV, vv)], inc=SDV)

            dn_stop = {}
            w0 = [(SAC, h_done)]
            for b_ in range(8):
                w0 += acq(b_)

            def dnfn(e, cc, fis):
                ins = None
                for fi in fis:
                    fc = cc * 4 + fi
                    for m in range(NM):
                        for H in range(2):
                            ins = e.matmul(
                                p[2 * m + H][:, 0:512],
                                h_fm[:, fc * 512 + m * 128:fc * 512 + m * 128 + 128],
                                dnsl[cc % 2][:, fi * 1024 + H * 512:fi * 1024 + H * 512 + 512],
                                start=(fc == 0), stop=False)
                return ins
            for cc in range(8):
                w_ = [(SWG, dnsup[cc])] + (w0 if cc == 0 else [])
                if cc < 7:
                    gv = E("t", lambda e, cc=cc: dnfn(e, cc, range(4)), waits=w_,
                           inc=SPE)
                    dnslfree[cc % 2] = (SPE, gv)
                    if cc + 2 < 8:
                        emit_dnsup(cc + 2)
                else:
                    E("t", lambda e, cc=cc: dnfn(e, cc, range(3)), waits=w_)
                    fc = FT - 1
                    gv = None
                    for m in range(NM):
                        for H in range(2):
                            gv = E("t", lambda e, m=m, H=H, fc=fc, cc=cc: e.matmul(
                                p[2 * m + H][:, 0:512],
                                h_fm[:, fc * 512 + m * 128:fc * 512 + m * 128 + 128],
                                dnsl[cc % 2][:, 3 * 1024 + H * 512:3 * 1024 + H * 512 + 512],
                                start=False, stop=True), inc=SPE)
                            dn_stop[(m, H)] = gv
                    dnslfree[cc % 2] = (SPE, gv)

            for m in range(NM):
                osl = m % 2
                w_ = [outfree[osl]] if outfree[osl] is not None else []
                fv = None
                for H in range(2):
                    bk = 2 * m + H
                    fv = E("v", lambda e, m=m, H=H, bk=bk, osl=osl: e.scalar_tensor_tensor(
                        outst[osl][:, H * 512:(H + 1) * 512], p[bk][:, 0:512],
                        b_s2[:, m:m + 1],
                        acc[:, m * 1024 + H * 512:m * 1024 + H * 512 + 512],
                        op0=ALU.mult, op1=ALU.add),
                        waits=[(SPE, dn_stop[(m, H)]), (SDV, s2v)] + w_, inc=SDV)
                    bankf[bk] = (SDV, fv)
                ov = E("s", lambda e, m=m, osl=osl, r0=r0: e.dma_start(
                    a_out[r0 + m * 128:r0 + (m + 1) * 128, :], outst[osl][:]),
                    waits=[(SDV, fv)], inc=SO, n=16)
                outfree[osl] = (SO, ov)

        for t in range(nt):
            emit_tile(t)

        if dbg:
            for nm, buf in [("d_xnfm", xn_fm), ("d_qfm", q_fm), ("d_kfm", k_fm),
                            ("d_vtm", v_tm), ("d_gO", sig_fm), ("d_hfm", h_fm),
                            ("d_stm", stm), ("d_acc", acc), ("d_xnffn", xnffn),
                            ("d_mv", b_mv)]:
                E("s", lambda e, nm=nm, buf=buf: e.dma_start(a_dbg[nm][:, :], buf[:]),
                  waits=[(SDV, SDV.v), (SAC, SAC.v), (SPE, SPE.v)], inc=SO, n=16)
            E("s", lambda e: e.dma_start(a_dbg["d_s12"][:, 0:4], b_s1[:, :]),
              inc=SO, n=16)
            E("s", lambda e: e.dma_start(a_dbg["d_s12"][:, 4:8], b_s2[:, :]),
              inc=SO, n=16)
            E("s", lambda e: e.dma_start(a_dbg["d_ln"][:, 0:4], b_rstd[:, :]), inc=SO, n=16)
            E("s", lambda e: e.dma_start(a_dbg["d_ln"][:, 4:8], b_b1[:, :]), inc=SO, n=16)
            E("s", lambda e: e.dma_start(a_dbg["d_ln"][:, 8:12], b_a3[:, :]), inc=SO, n=16)

        E("s", lambda e: e.wait_ge(SO.h, SO.v))

        block = E_(nc.Block())

        def run_stream(key):
            def emitfn(e):
                for wl, fn, semh, n in prog[key]:
                    for sh, v in wl:
                        e.wait_ge(sh, v)
                    ins = fn(e)
                    if semh is not None:
                        assert ins is not None
                        ins.then_inc(semh, n)
            return emitfn

        block.tensor(run_stream("t"))
        block.vector(run_stream("v"))
        block.scalar(run_stream("s"))
        block.sync(run_stream("y"))
        block.gpsimd(run_stream("g"))
    return nc


def _prep_inputs(inp):
    f32 = np.float32
    x = np.ascontiguousarray(np.asarray(inp["L0_attn_x"], f32))
    d = {}
    for nm, key in [("q", "L0_attn_W_q"), ("k", "L0_attn_W_k"), ("g", "L0_attn_W_g"),
                    ("v", "L0_attn_W_v"), ("o", "L0_attn_W_o")]:
        d[f"w{nm}T"] = np.ascontiguousarray(np.asarray(inp[key], f32).T)
    d["wupT"] = np.ascontiguousarray(np.asarray(inp["L0_ffn_w_up"], f32).T)
    d["wdnT"] = np.ascontiguousarray(np.asarray(inp["L0_ffn_w_down"], f32).T)
    d["ld"] = np.ascontiguousarray(
        np.asarray(inp["L0_attn_log_decay"], f32)[0:1].reshape(1, 1))
    d["identf"] = np.eye(128, dtype=f32)
    d["onesrow"] = np.ones((1, 128), f32)
    d["onescol"] = np.ones((128, 1), ml_dtypes.bfloat16)
    cst = np.zeros((128, 4), f32)
    cst[:, 0] = 1e-5
    d["cst"] = cst
    idx = np.arange(128)
    diff = idx[None, :] - idx[:, None]
    valid = ((idx[:, None] // 32) == (idx[None, :] // 32)) & (diff >= 0)
    d["maskE"] = np.where(valid, diff, 0).astype(f32)
    d["maskZ"] = valid.astype(f32)
    return x, d


def kernel(**inputs):
    x, shared = _prep_inputs(inputs)
    nc = build_program()
    in_maps = []
    for c in range(NCORE):
        m = dict(shared)
        m["x"] = np.ascontiguousarray(x[c])
        in_maps.append(m)
    res = run_bass_kernel_spmd(nc, in_maps, core_ids=list(range(NCORE)))
    return np.stack([r["out"] for r in res.results], axis=0)
